# version 02
# baseline (speedup 1.0000x reference)
import contextlib

import numpy as np

import concourse.bass as bass
import concourse.mybir as mybir
from concourse.bass_utils import run_bass_kernel_spmd

F32 = mybir.dt.float32
BF16 = mybir.dt.bfloat16
ALU = mybir.AluOpType
ACT = mybir.ActivationFunctionType

LN_EPS = 1e-5


class Cfg:
    def __init__(self, D=2048, T=2048, TT=512, S=64, I=8192, IG=8, debug=False):
        self.D, self.T, self.TT, self.S, self.I = D, T, TT, S, I
        self.NK = D // 128
        self.TO = T // 2
        self.NTT = T // TT
        self.NTO = self.TO // TT
        self.NI = I // 128
        self.IG = IG
        self.NG = self.NI // IG
        self.JM = 6 * D
        self.NJT = self.JM // TT
        self.NJ2 = self.JM // 128
        self.debug = debug
        assert self.NTO * TT == self.TO and self.NG * IG == self.NI
        assert self.NJ2 <= 128 and self.NK <= 32


class B:

    ENGINES = ("sync", "tensor", "vector", "scalar", "gpsimd")

    def __init__(self, nc):
        self.nc = nc
        self.sems = {}
        self.cnt = {}
        self.waited = {}
        self.phase = []

    def make_sems(self, stack, names):
        for n in names:
            self.sems[n] = stack.enter_context(self.nc.semaphore(n))
            self.cnt[n] = 0

    def need(self, engine, deps):
        for s, v in deps:
            if v is None or v <= 0:
                continue
            key = (engine, s)
            if self.waited.get(key, 0) >= v:
                continue
            self.waited[key] = v
            sem = self.sems[s]
            self.phase.append((engine, lambda e, sem=sem, v=v: e.wait_ge(sem, v)))

    def op(self, engine, fn, deps=(), inc=None, inc_by=1):
        deps = list(deps)
        if engine == "vector" and self.cnt.get("V", 0) > 0:
            deps.append(("V", self.cnt["V"]))
        if engine == "scalar" and self.cnt.get("SC", 0) > 0:
            deps.append(("SC", self.cnt["SC"]))
        self.need(engine, deps)
        if inc is None:
            self.phase.append((engine, fn))
            return None
        sem = self.sems[inc]
        self.cnt[inc] += inc_by
        val = self.cnt[inc]
        self.phase.append(
            (engine, lambda e, fn=fn, sem=sem, n=inc_by: fn(e).then_inc(sem, n)))
        return val

    def run_phase(self, name):
        sched, self.phase = self.phase, []
        if not sched:
            return
        with self.nc.Block(name) as block:
            for eng in self.ENGINES:
                items = [f for e, f in sched if e == eng]
                if items:
                    getattr(block, eng)(
                        lambda engine, items=items: [f(engine) for f in items])


def build_program(c: Cfg):
    nc = bass.Bass("TRN2", target_bir_lowering=False)
    D, T, TT, S, NK, TO, NTT, NTO = c.D, c.T, c.TT, c.S, c.NK, c.TO, c.NTT, c.NTO

    xT = nc.dram_tensor("xT", [D, T], F32, kind="ExternalInput").ap()
    wshiftT = nc.dram_tensor("wshiftT", [D, D], F32, kind="ExternalInput").ap()
    wkeyT = nc.dram_tensor("wkeyT", [D, S], F32, kind="ExternalInput").ap()
    wvalT = nc.dram_tensor("wvalT", [D, S], F32, kind="ExternalInput").ap()
    woutT = nc.dram_tensor("woutT", [S, D], F32, kind="ExternalInput").ap()
    gwT = nc.dram_tensor("gwT", [D, c.I], F32, kind="ExternalInput").ap()
    upT = nc.dram_tensor("upT", [D, c.I], F32, kind="ExternalInput").ap()
    dwT = nc.dram_tensor("dwT", [c.I, D], F32, kind="ExternalInput").ap()
    wmod_h = nc.dram_tensor("wmod", [D, c.JM], F32, kind="ExternalInput").ap()
    gatepp = nc.dram_tensor("gatepp", [128, NK], F32, kind="ExternalInput").ap()
    lngpp = nc.dram_tensor("lngpp", [128, NK], F32, kind="ExternalInput").ap()
    lnbpp = nc.dram_tensor("lnbpp", [128, NK], F32, kind="ExternalInput").ap()
    cpp = nc.dram_tensor("cpp", [128, NK], F32, kind="ExternalInput").ap()
    td_h = nc.dram_tensor("td", [S, 1], F32, kind="ExternalInput").ap()
    tf_h = nc.dram_tensor("tf", [S, 1], F32, kind="ExternalInput").ap()
    cmask_h = nc.dram_tensor("cmask", [S, 1], F32, kind="ExternalInput").ap()
    outT_h = nc.dram_tensor("outT", [D, TO], F32, kind="ExternalOutput").ap()
    dbg = {}
    if c.debug:
        for nm, shp, dt in [
            ("dbg_modpp", [128, 6 * NK], F32),
            ("dbg_hb", [128, NK * T], BF16),
            ("dbg_xshift", [128, NK * T], BF16),
            ("dbg_kall", [S, T], F32),
            ("dbg_wkv", [S, T], F32),
            ("dbg_states", [S, TO], F32),
            ("dbg_xmid", [128, NK * TO], F32),
            ("dbg_h2", [128, NK * TO], BF16),
        ]:
            dbg[nm] = nc.dram_tensor(nm, shp, dt, kind="ExternalOutput").ap()

    b = B(nc)
    st_top = contextlib.ExitStack()
    st_hb = contextlib.ExitStack()
    st_xs = contextlib.ExitStack()
    st_xm = contextlib.ExitStack()
    with st_top:
        b.make_sems(st_top, ["V", "SC", "P", "Q", "GA", "GB", "GC", "GD", "HX", "HY", "HO"])
        sb = lambda st, name, shp, dt, side=None: st.enter_context(
            nc.sbuf_tensor(name, shp, dt, side=side) if side else
            nc.sbuf_tensor(name, shp, dt))
        ps = [st_top.enter_context(nc.psum_tensor(f"ps{i}", [128, TT], F32))
              for i in range(8)]
        bank_free = [("Q", 0)] * 8
        bank_ready = [0] * 8

        g_sig = sb(st_top, "g_sig", [128, NK], F32)
        one_m_g = sb(st_top, "one_m_g", [128, NK], F32)
        lng = sb(st_top, "lng", [128, NK], F32)
        lnb = sb(st_top, "lnb", [128, NK], F32)
        u_neg = sb(st_top, "u_neg", [S, 1], F32)
        w_dec = sb(st_top, "w_dec", [S, 1], F32)
        cmask = sb(st_top, "cmask_s", [S, 1], F32)
        modpp = sb(st_top, "modpp", [128, 6 * NK], F32)
        ca1 = sb(st_top, "ca1", [128, NK], F32)
        cb1 = sb(st_top, "cb1", [128, NK], F32)
        ga = sb(st_top, "ga", [128, NK], F32)
        gb = sb(st_top, "gb", [128, NK], F32)
        ones_bf = sb(st_top, "ones_bf", [128, 1], BF16)
        ones16 = sb(st_top, "ones16", [128, 1], F32)
        onesr = sb(st_top, "onesr", [1, 128], F32)
        stat = sb(st_top, "stat", [1, 6 * TT], F32)
        MU, M2, MSQ, VAR, SD, RINV = range(6)
        stv = lambda i: stat[0:1, i * TT:(i + 1) * TT]
        sq_scr = [sb(st_top, f"sq{i}", [128, TT], BF16) for i in range(2)]
        f_scr = [sb(st_top, f"fs{i}", [128, TT], F32) for i in range(4)]

        with contextlib.ExitStack() as ph:
            gpv = sb(ph, "gpv", [128, NK], F32)
            tdv = sb(ph, "tdv", [S, 1], F32)
            tfv = sb(ph, "tfv", [S, 1], F32)
            cvv = sb(ph, "cvv", [128, NK], F32)
            silu_c = sb(ph, "silu_c", [128, NK], F32)
            silu_cb = sb(ph, "silu_cb", [128, NK], BF16)
            diag = sb(ph, "diag", [128, NK * NK], BF16)
            iden = sb(ph, "iden", [128, 128], F32)
            wmbuf = [sb(ph, f"wmbuf{i}", [128, 8 * TT], BF16) for i in range(2)]
            stackb = sb(ph, "stackb", [NK, c.JM], F32)
            mrow = sb(ph, "mrow", [1, c.JM], F32)
            mtmp = sb(ph, "mtmp", [c.NJ2, 128], F32)

            for dst, src in [(gpv, gatepp), (tdv, td_h), (tfv, tf_h), (cvv, cpp),
                             (cmask, cmask_h), (lng, lngpp), (lnb, lnbpp)]:
                b.op("gpsimd", lambda e, d=dst, s=src: e.dma_start(out=d[:, :], in_=s[:, :]),
                     inc="GC", inc_by=16)
            v_gc = b.cnt["GC"]
            for t_, v_ in [(ones_bf, 1.0), (ones16, 1.0), (onesr, 1.0)]:
                b.op("gpsimd", lambda e, t=t_, v=v_: e.memset(t[:, :], v), inc="Q")
            b.op("gpsimd", lambda e: e.memset(diag[:, :], 0.0), inc="Q")
            b.op("gpsimd", lambda e: e.iota(iden[:, :], [[1, 128]], channel_multiplier=-1,
                                            allow_small_or_imprecise_dtypes=True), inc="Q")
            v_q = b.cnt["Q"]
            v_id = b.op("vector", lambda e: e.tensor_scalar(
                out=iden[:, :], in0=iden[:, :], scalar1=0.0, scalar2=None,
                op0=ALU.is_equal), deps=[("Q", v_q)], inc="V")
            v_sig = b.op("scalar", lambda e: e.activation(out=g_sig[:, :], in_=gpv[:, :],
                         func=ACT.Sigmoid), deps=[("GC", v_gc)], inc="SC")
            b.op("vector", lambda e: e.tensor_scalar(
                out=one_m_g[:, :], in0=g_sig[:, :], scalar1=-1.0, scalar2=1.0,
                op0=ALU.mult, op1=ALU.add), deps=[("SC", v_sig)], inc="V")
            v_u = b.op("scalar", lambda e: e.activation(out=u_neg[:, :], in_=tfv[:, :],
                       func=ACT.Exp), deps=[("GC", v_gc)], inc="SC")
            b.op("vector", lambda e: e.tensor_scalar(
                out=u_neg[:, :], in0=u_neg[:, :], scalar1=-1.0, scalar2=None,
                op0=ALU.mult), deps=[("SC", v_u)], inc="V")
            b.op("scalar", lambda e: e.activation(out=w_dec[:, :], in_=tdv[:, :],
                 func=ACT.Exp), deps=[("GC", v_gc)], inc="SC")
            v_silu = b.op("scalar", lambda e: e.activation(out=silu_c[:, :], in_=cvv[:, :],
                          func=ACT.Silu), deps=[("GC", v_gc)], inc="SC")
            v_scb = b.op("vector", lambda e: e.tensor_copy(out=silu_cb[:, :], in_=silu_c[:, :]),
                         deps=[("SC", v_silu)], inc="V")
            for k in range(NK):
                b.op("vector", lambda e, k=k: e.tensor_copy(
                    out=diag[:, k * NK + k:k * NK + k + 1], in_=silu_cb[:, k:k + 1]),
                    deps=[("V", v_scb), ("Q", v_q)], inc="V")
            v_diag = b.cnt["V"]

            gen_done_P = {}
            kgen = 0
            j0 = 0
            while j0 < c.NJT:
                wave = min(8, c.NJT - j0)
                for k in range(NK):
                    buf = wmbuf[kgen % 2]
                    gsem = "GA" if kgen % 2 == 0 else "GB"
                    vdma = b.op("gpsimd", lambda e, k=k, j0=j0, w=wave, buf=buf:
                                e.dma_start(out=buf[:, 0:w * TT],
                                            in_=wmod_h[k * 128:(k + 1) * 128,
                                                       j0 * TT:(j0 + w) * TT]),
                                deps=[("P", gen_done_P.get(kgen % 2, 0))],
                                inc=gsem, inc_by=16)
                    mmdeps = [(gsem, vdma), ("V", v_diag)]
                    if k == 0:
                        for j in range(wave):
                            mmdeps.append(bank_free[j])
                    vmm = None
                    for j in range(wave):
                        vmm = b.op("tensor", lambda e, k=k, j=j, j0=j0, buf=buf: e.matmul(
                            ps[j][0:NK, 0:TT], diag[:, k * NK:(k + 1) * NK],
                            buf[:, j * TT:(j + 1) * TT],
                            start=(k == 0), stop=(k == NK - 1)),
                            deps=mmdeps, inc="P")
                        mmdeps = []
                    gen_done_P[kgen % 2] = vmm
                    kgen += 1
                for j in range(wave):
                    bank_ready[j] = b.cnt["P"]
                    vev = b.op("vector", lambda e, j=j, j0=j0: e.tensor_copy(
                        out=stackb[0:NK, (j0 + j) * TT:(j0 + j + 1) * TT],
                        in_=ps[j][0:NK, 0:TT]),
                        deps=[("P", bank_ready[j])], inc="V")
                    bank_free[j] = ("V", vev)
                j0 += wave
            for j in range(c.NJT):
                bank = 6 if j % 2 == 0 else 7
                vmm = b.op("tensor", lambda e, j=j, bank=bank: e.matmul(
                    ps[bank][0:1, 0:TT], ones16[0:NK, 0:1],
                    stackb[0:NK, j * TT:(j + 1) * TT], start=True, stop=True),
                    deps=[bank_free[bank], ("V", b.cnt["V"])], inc="P")
                bank_ready[bank] = vmm
                vev = b.op("vector", lambda e, j=j, bank=bank: e.tensor_copy(
                    out=mrow[0:1, j * TT:(j + 1) * TT], in_=ps[bank][0:1, 0:TT]),
                    deps=[("P", vmm)], inc="V")
                bank_free[bank] = ("V", vev)
            v_mrow = b.cnt["V"]
            v_rel = b.op("gpsimd", lambda e: e.dma_start(
                out=mtmp[:, :],
                in_=mrow[0:1, :].rearrange("p (a x) -> p a x", x=128)),
                deps=[("V", v_mrow)], inc="GC", inc_by=16)
            v_tr = b.op("tensor", lambda e: e.transpose(
                ps[0][:, 0:c.NJ2], mtmp[:, :], iden[0:c.NJ2, 0:c.NJ2]),
                deps=[("GC", v_rel), ("V", v_id), bank_free[0]], inc="P")
            bank_ready[0] = v_tr
            v_mpp = b.op("scalar", lambda e: e.copy(out=modpp[:, :], in_=ps[0][:, 0:c.NJ2]),
                         deps=[("P", v_tr)], inc="SC")
            bank_free[0] = ("SC", v_mpp)
            b.op("vector", lambda e: e.tensor_scalar(
                out=ca1[:, :], in0=modpp[:, NK:2 * NK], scalar1=1.0, scalar2=None,
                op0=ALU.add), deps=[("SC", v_mpp)], inc="V")
            b.op("vector", lambda e: e.tensor_scalar(
                out=cb1[:, :], in0=modpp[:, 4 * NK:5 * NK], scalar1=1.0, scalar2=None,
                op0=ALU.add), deps=[("SC", v_mpp)], inc="V")
            for dst, sl in [(ga, 2), (gb, 5)]:
                vge = b.op("vector", lambda e, d=dst, sl=sl: e.tensor_scalar(
                    out=d[:, :], in0=modpp[:, sl * NK:(sl + 1) * NK], scalar1=0.0,
                    scalar2=-1.0 if False else None, op0=ALU.is_ge),
                    deps=[("SC", v_mpp)], inc="V")
                vsg = b.op("vector", lambda e, d=dst: e.tensor_scalar(
                    out=d[:, :], in0=d[:, :], scalar1=2.0, scalar2=-1.0,
                    op0=ALU.mult, op1=ALU.add), deps=[("V", vge)], inc="V")
                vab = b.op("scalar", lambda e, sl=sl, i=sl: e.activation(
                    out=f_scr[0][:, 0:NK] if sl == 2 else f_scr[1][:, 0:NK],
                    in_=modpp[:, sl * NK:(sl + 1) * NK], func=ACT.Abs),
                    deps=[("SC", v_mpp)], inc="SC")
                vmx = b.op("vector", lambda e, sl=sl: e.tensor_scalar(
                    out=f_scr[0][:, 0:NK] if sl == 2 else f_scr[1][:, 0:NK],
                    in0=f_scr[0][:, 0:NK] if sl == 2 else f_scr[1][:, 0:NK],
                    scalar1=0.25, scalar2=None, op0=ALU.max),
                    deps=[("SC", vab), ("V", vsg)], inc="V")
                b.op("vector", lambda e, d=dst, sl=sl: e.tensor_tensor(
                    out=d[:, :], in0=d[:, :],
                    in1=f_scr[0][:, 0:NK] if sl == 2 else f_scr[1][:, 0:NK],
                    op=ALU.mult), deps=[("V", vmx)], inc="V")
            if c.debug:
                vv = b.cnt["V"]
                b.op("sync", lambda e: e.dma_start(out=dbg["dbg_modpp"][:, :],
                     in_=modpp[:, :]), deps=[("V", vv)], inc="HO", inc_by=16)
                b.need("sync", [("HO", b.cnt["HO"])])
            b.run_phase("p0_mod")

        def ln_stats_and_bcast(get_xbf, extra_dep, banks, n_chunks=NK):
            bSX, bSX2, bBM, bBR = banks
            sqdep = [0, 0]
            dep0 = list(extra_dep)
            for k in range(n_chunks):
                scr = sq_scr[k % 2]
                vsq = b.op("scalar", lambda e, k=k, scr=scr: e.activation(
                    out=scr[:, :], in_=get_xbf(k), func=ACT.Square),
                    deps=dep0 + [("P", sqdep[k % 2])], inc="SC")
                mmdeps = dep0 + [("SC", vsq)]
                if k == 0:
                    mmdeps += [bank_free[bSX], bank_free[bSX2]]
                b.op("tensor", lambda e, k=k, first=(k == 0), last=(k == n_chunks - 1):
                     e.matmul(ps[bSX][0:1, 0:TT], ones_bf[:, :], get_xbf(k),
                              start=first, stop=last), deps=mmdeps, inc="P")
                vmm2 = b.op("tensor", lambda e, k=k, scr=scr,
                            first=(k == 0), last=(k == n_chunks - 1):
                            e.matmul(ps[bSX2][0:1, 0:TT], ones_bf[:, :], scr[:, :],
                                     start=first, stop=last), inc="P")
                sqdep[k % 2] = vmm2
            bank_ready[bSX] = bank_ready[bSX2] = b.cnt["P"]
            pst = b.cnt["P"]
            v1 = b.op("vector", lambda e: e.tensor_scalar(
                out=stv(MU), in0=ps[bSX][0:1, 0:TT], scalar1=1.0 / D, scalar2=None,
                op0=ALU.mult), deps=[("P", pst)], inc="V")
            v2 = b.op("vector", lambda e: e.tensor_scalar(
                out=stv(M2), in0=ps[bSX2][0:1, 0:TT], scalar1=1.0 / D, scalar2=None,
                op0=ALU.mult), deps=[("P", pst)], inc="V")
            bank_free[bSX] = ("V", v1)
            bank_free[bSX2] = ("V", v2)
            v3 = b.op("vector", lambda e: e.tensor_tensor(
                out=stv(MSQ), in0=stv(MU), in1=stv(MU), op=ALU.mult),
                deps=[("V", v1)], inc="V")
            v4 = b.op("vector", lambda e: e.scalar_tensor_tensor(
                out=stv(VAR), in0=stv(M2), scalar=LN_EPS, in1=stv(MSQ),
                op0=ALU.add, op1=ALU.subtract), deps=[("V", v3)], inc="V")
            v5 = b.op("scalar", lambda e: e.activation(
                out=stv(SD), in_=stv(VAR), func=ACT.Sqrt),
                deps=[("V", v4)], inc="SC")
            v6 = b.op("vector", lambda e: e.reciprocal(out=stv(RINV), in_=stv(SD)),
                      deps=[("SC", v5)], inc="V")
            vbm = b.op("tensor", lambda e: e.matmul(
                ps[bBM][:, 0:TT], onesr[:, :], stv(MU), start=True, stop=True),
                deps=[("V", v1), bank_free[bBM]], inc="P")
            vbr = b.op("tensor", lambda e: e.matmul(
                ps[bBR][:, 0:TT], onesr[:, :], stv(RINV), start=True, stop=True),
                deps=[("V", v6), bank_free[bBR]], inc="P")
            bank_ready[bBM], bank_ready[bBR] = vbm, vbr
            return bBM, bBR, vbm, vbr

        def ln_apply(x_ap, out_ap, bBM, bBR, pdep, sc_pp, bias_pp, scr):
            va = b.op("vector", lambda e: e.tensor_tensor(
                out=scr[:, :], in0=x_ap, in1=ps[bBM][:, 0:TT], op=ALU.subtract),
                deps=pdep, inc="V")
            vb = b.op("vector", lambda e: e.tensor_tensor(
                out=scr[:, :], in0=scr[:, :], in1=ps[bBR][:, 0:TT], op=ALU.mult),
                deps=[("V", va)], inc="V")
            vc = b.op("vector", lambda e: e.tensor_scalar(
                out=out_ap, in0=scr[:, :], scalar1=sc_pp, scalar2=bias_pp,
                op0=ALU.mult, op1=ALU.add), deps=[("V", vb)], inc="V")
            return vc

        hb = sb(st_hb, "hb", [128, NK * T], BF16)
        with contextlib.ExitStack() as ph:
            xb = sb(ph, "xb", [128, NK * T], BF16)
            for k in range(NK):
                b.op("gpsimd", lambda e, k=k: e.dma_start(
                    out=xb[:, k * T:(k + 1) * T], in_=xT[k * 128:(k + 1) * 128, :]),
                    inc="GC", inc_by=16)
            v_xb = b.cnt["GC"]
            for ti in range(NTT):
                banks = (0, 1, 2, 3) if ti % 2 == 0 else (4, 5, 6, 7)
                getx = lambda k, ti=ti: xb[:, k * T + ti * TT:k * T + ti * TT + TT]
                bBM, bBR, vbm, vbr = ln_stats_and_bcast(getx, [("GC", v_xb)], banks)
                lastap = 0
                for k in range(NK):
                    lastap = ln_apply(
                        getx(k), hb[:, k * T + ti * TT:k * T + ti * TT + TT],
                        bBM, bBR, [("P", max(vbm, vbr))],
                        ca1[:, k:k + 1], modpp[:, 0 * NK + k:0 * NK + k + 1],
                        f_scr[k % 2])
                bank_free[bBM] = bank_free[bBR] = ("V", lastap)
            if c.debug:
                vv = b.cnt["V"]
                b.op("sync", lambda e: e.dma_start(out=dbg["dbg_hb"][:, :], in_=hb[:, :]),
                     deps=[("V", vv)], inc="HO", inc_by=16)
                b.need("sync", [("HO", b.cnt["HO"])])
            b.run_phase("p1_h")

        x_shift = sb(st_xs, "x_shift", [128, NK * T], BF16, side="right")
        with contextlib.ExitStack() as ph:
            wsb = [sb(ph, f"wsb{i}", [128, NK * 128], BF16) for i in range(2)]
            gen_done = {0: 0, 1: 0}
            for e_i in range(NK):
                buf = wsb[e_i % 2]
                gsem = "GA" if e_i % 2 == 0 else "GB"
                vdma = b.op("gpsimd", lambda e, e_i=e_i, buf=buf: e.dma_start(
                    out=buf[:, :].rearrange("p (k x) -> p k x", x=128),
                    in_=wshiftT[:, e_i * 128:(e_i + 1) * 128]
                    .rearrange("(k p) x -> p k x", p=128)),
                    deps=[("P", gen_done[e_i % 2])], inc=gsem, inc_by=16)
                for ti in range(NTT):
                    bank = (e_i * NTT + ti) % 2
                    src_t = ((ti + NTT // 2) % NTT) * TT
                    mmdeps = [(gsem, vdma), bank_free[bank]]
                    vmm = None
                    for k in range(NK):
                        vmm = b.op("tensor", lambda e, k=k, e_i=e_i, buf=buf, bank=bank,
                                   src_t=src_t, first=(k == 0), last=(k == NK - 1):
                                   e.matmul(ps[bank][:, 0:TT],
                                            buf[:, k * 128:(k + 1) * 128],
                                            hb[:, k * T + src_t:k * T + src_t + TT],
                                            start=first, stop=last),
                                   deps=mmdeps, inc="P")
                        mmdeps = []
                    bank_ready[bank] = vmm
                    gen_done[e_i % 2] = vmm
                    scr = f_scr[2 + (ti % 2)]
                    vh = b.op("vector", lambda e, e_i=e_i, ti=ti, scr=scr: e.tensor_scalar(
                        out=scr[:, :], in0=hb[:, e_i * T + ti * TT:e_i * T + ti * TT + TT],
                        scalar1=one_m_g[:, e_i:e_i + 1], scalar2=None, op0=ALU.mult),
                        inc="V")
                    vev = b.op("vector", lambda e, e_i=e_i, ti=ti, bank=bank, scr=scr:
                               e.scalar_tensor_tensor(
                                   out=x_shift[:, e_i * T + ti * TT:e_i * T + ti * TT + TT],
                                   in0=ps[bank][:, 0:TT], scalar=g_sig[:, e_i:e_i + 1],
                                   in1=scr[:, :], op0=ALU.mult, op1=ALU.add),
                               deps=[("P", vmm), ("V", vh)], inc="V")
                    bank_free[bank] = ("V", vev)
            if c.debug:
                vv = b.cnt["V"]
                b.op("sync", lambda e: e.dma_start(out=dbg["dbg_xshift"][:, :],
                     in_=x_shift[:, :]), deps=[("V", vv)], inc="HO", inc_by=16)
                b.need("sync", [("HO", b.cnt["HO"])])
            b.run_phase("p2_shift")
        st_hb.close()

        states_bf = sb(st_xm, "states_bf", [S, TO], BF16)
        with contextlib.ExitStack() as ph:
            kv_all = sb(ph, "kv_all", [S, 2 * T], F32)
            wkv = sb(ph, "wkv", [S, T], F32)
            wk_sb = sb(ph, "wk_sb", [128, NK * S], BF16)
            wv_sb = sb(ph, "wv_sb", [128, NK * S], BF16)
            xn = sb(ph, "xn", [128, NK * TT], BF16)
            b.op("gpsimd", lambda e: e.dma_start(
                out=wk_sb[:, :].rearrange("p (k s) -> p k s", s=S),
                in_=wkeyT[:, :].rearrange("(k p) s -> p k s", p=128)),
                inc="GC", inc_by=16)
            b.op("gpsimd", lambda e: e.dma_start(
                out=wv_sb[:, :].rearrange("p (k s) -> p k s", s=S),
                in_=wvalT[:, :].rearrange("(k p) s -> p k s", p=128)),
                inc="GC", inc_by=16)
            v_wkv = b.cnt["GC"]
            for ti in range(NTT):
                getx = lambda k, ti=ti: x_shift[:, k * T + ti * TT:k * T + ti * TT + TT]
                bBM, bBR, vbm, vbr = ln_stats_and_bcast(getx, [], (0, 1, 2, 3))
                lastap = 0
                for k in range(NK):
                    lastap = ln_apply(
                        getx(k), xn[:, k * TT:(k + 1) * TT], bBM, bBR,
                        [("P", max(vbm, vbr))],
                        lng[:, k:k + 1], lnb[:, k:k + 1], f_scr[k % 2])
                bank_free[bBM] = bank_free[bBR] = ("V", lastap)
                for out_i, w_t in [(0, wk_sb), (1, wv_sb)]:
                    bank = 4 + out_i
                    mmdeps = [("GC", v_wkv), ("V", lastap), bank_free[bank]]
                    vmm = None
                    for k in range(NK):
                        vmm = b.op("tensor", lambda e, k=k, w_t=w_t, bank=bank,
                                   first=(k == 0), last=(k == NK - 1):
                                   e.matmul(ps[bank][0:S, 0:TT],
                                            w_t[:, k * S:(k + 1) * S],
                                            xn[:, k * TT:(k + 1) * TT],
                                            start=first, stop=last),
                                   deps=mmdeps, inc="P")
                        mmdeps = []
                    bank_ready[bank] = vmm
                    vev = b.op("vector", lambda e, ti=ti, out_i=out_i, bank=bank:
                               e.tensor_copy(
                                   out=kv_all[:, out_i * T + ti * TT:out_i * T + ti * TT + TT],
                                   in_=ps[bank][0:S, 0:TT]),
                               deps=[("P", vmm)], inc="V")
                    bank_free[bank] = ("V", vev)
            v_kv = b.cnt["V"]
            vsg = b.op("scalar", lambda e: e.activation(
                out=wkv[:, :], in_=kv_all[:, 0:T], func=ACT.Sigmoid),
                deps=[("V", v_kv)], inc="SC")
            vts = b.op("vector", lambda e: e.tensor_scalar(
                out=wkv[:, :], in0=wkv[:, :], scalar1=u_neg[:, :], scalar2=None,
                op0=ALU.mult), deps=[("SC", vsg)], inc="V")
            vex = b.op("scalar", lambda e: e.activation(
                out=wkv[:, :], in_=wkv[:, :], func=ACT.Exp),
                deps=[("V", vts)], inc="SC")
            v_wkvd = b.op("vector", lambda e: e.tensor_tensor(
                out=wkv[:, :], in0=wkv[:, :], in1=kv_all[:, T:2 * T], op=ALU.mult),
                deps=[("SC", vex)], inc="V")
            if c.debug:
                b.op("sync", lambda e: e.dma_start(out=dbg["dbg_kall"][:, :],
                     in_=kv_all[:, 0:T]), deps=[("V", v_kv)], inc="HO", inc_by=16)
                b.need("sync", [("HO", b.cnt["HO"])])
                b.op("sync", lambda e: e.dma_start(out=dbg["dbg_wkv"][:, :],
                     in_=wkv[:, :]), deps=[("V", v_wkvd)], inc="HO", inc_by=16)
                b.need("sync", [("HO", b.cnt["HO"])])
            wrep = sb(ph, "wrep", [S, T], F32)
            scanb = sb(ph, "scanb", [S, TO], F32)
            carry = sb(ph, "carry", [S, 1], F32)
            states = sb(ph, "states", [S, TO], F32)
            vwr = b.op("vector", lambda e: e.tensor_scalar(
                out=wrep[:, :], in0=wkv[:, :], scalar1=0.0, scalar2=w_dec[:, :],
                op0=ALU.mult, op1=ALU.add), deps=[("V", v_wkvd)], inc="V")
            vs1 = b.op("vector", lambda e: e.tensor_tensor_scan(
                out=scanb[:, :], data0=wrep[:, TO:T], data1=wkv[:, TO:T],
                initial=0.0, op0=ALU.mult, op1=ALU.add),
                deps=[("V", vwr)], inc="V")
            vc1 = b.op("vector", lambda e: e.tensor_tensor(
                out=carry[:, :], in0=scanb[:, TO - 1:TO], in1=cmask[:, :],
                op=ALU.mult), deps=[("V", vs1)], inc="V")
            vs2 = b.op("vector", lambda e: e.tensor_tensor_scan(
                out=states[:, :], data0=wrep[:, 0:TO], data1=wkv[:, 0:TO],
                initial=carry[:, :], op0=ALU.mult, op1=ALU.add),
                deps=[("V", vc1)], inc="V")
            v_stb = b.op("vector", lambda e: e.tensor_copy(
                out=states_bf[:, :], in_=states[:, :]), deps=[("V", vs2)], inc="V")
            if c.debug:
                b.op("sync", lambda e: e.dma_start(out=dbg["dbg_states"][:, :],
                     in_=states[:, :]), deps=[("V", vs2)], inc="HO", inc_by=16)
                b.need("sync", [("HO", b.cnt["HO"])])
            b.run_phase("p3a_kv")
        st_xs.close()

        xmid = sb(st_xm, "xmid", [128, NK * TO], F32)
        with contextlib.ExitStack() as ph:
            wo_sb2 = sb(ph, "wo_sb2", [S, D], BF16)
            xo = [sb(ph, f"xo{i}", [128, TO], F32) for i in range(2)]
            b.op("gpsimd", lambda e: e.dma_start(out=wo_sb2[:, :], in_=woutT[:, :]),
                 inc="GC", inc_by=16)
            v_wo = b.cnt["GC"]
            xo_done = {0: 0, 1: 0}
            for k in range(NK):
                buf = xo[k % 2]
                hxs = "HX" if k % 2 == 0 else "HY"
                vdma = b.op("sync", lambda e, k=k, buf=buf: e.dma_start(
                    out=buf[:, :], in_=xT[k * 128:(k + 1) * 128, 0:TO]),
                    deps=[("V", xo_done[k % 2])], inc=hxs, inc_by=16)
                for t2 in range(NTO):
                    bank = (k * NTO + t2) % 2
                    vmm = b.op("tensor", lambda e, k=k, t2=t2, bank=bank: e.matmul(
                        ps[bank][:, 0:TT], wo_sb2[0:S, k * 128:(k + 1) * 128],
                        states_bf[:, t2 * TT:(t2 + 1) * TT], start=True, stop=True),
                        deps=[("GC", v_wo), bank_free[bank]], inc="P")
                    bank_ready[bank] = vmm
                    vev = b.op("vector", lambda e, k=k, t2=t2, bank=bank, buf=buf:
                               e.scalar_tensor_tensor(
                                   out=xmid[:, k * TO + t2 * TT:k * TO + t2 * TT + TT],
                                   in0=ps[bank][:, 0:TT], scalar=ga[:, k:k + 1],
                                   in1=buf[:, t2 * TT:(t2 + 1) * TT],
                                   op0=ALU.mult, op1=ALU.add),
                               deps=[("P", vmm), (hxs, vdma)], inc="V")
                    bank_free[bank] = ("V", vev)
                    xo_done[k % 2] = vev
            if c.debug:
                vv = b.cnt["V"]
                b.op("sync", lambda e: e.dma_start(out=dbg["dbg_xmid"][:, :],
                     in_=xmid[:, :]), deps=[("V", vv)], inc="HO", inc_by=16)
                b.need("sync", [("HO", b.cnt["HO"])])
            b.run_phase("p3b_res")

        h2 = sb(st_xm, "h2", [128, NK * TO], BF16)
        with contextlib.ExitStack() as ph:
            xmb = [sb(ph, f"xmb{i}", [128, TT], BF16) for i in range(2)]
            xmb_sc = {0: 0, 1: 0}
            xmb_p = {0: 0, 1: 0}

            def emit_ln3_tile(t2, banks):
                sqdep = [0, 0]
                bSX, bSX2, bBM, bBR = banks
                for k in range(NK):
                    scr = sq_scr[k % 2]
                    vcp = b.op("vector", lambda e, k=k, t2=t2, buf=xmb[k % 2]:
                               e.tensor_copy(
                                   out=buf[:, :],
                                   in_=xmid[:, k * TO + t2 * TT:k * TO + t2 * TT + TT]),
                               deps=[("SC", xmb_sc[k % 2]), ("P", xmb_p[k % 2])],
                               inc="V")
                    vsq = b.op("scalar", lambda e, k=k, scr=scr, buf=xmb[k % 2]:
                               e.activation(out=scr[:, :], in_=buf[:, :], func=ACT.Square),
                               deps=[("V", vcp), ("P", sqdep[k % 2])], inc="SC")
                    xmb_sc[k % 2] = vsq
                    mmdeps = [("SC", vsq)]
                    if k == 0:
                        mmdeps += [bank_free[bSX], bank_free[bSX2]]
                    vmm1 = b.op("tensor", lambda e, k=k, buf=xmb[k % 2],
                                first=(k == 0), last=(k == NK - 1):
                                e.matmul(ps[bSX][0:1, 0:TT], ones_bf[:, :], buf[:, :],
                                         start=first, stop=last),
                                deps=mmdeps + [("V", vcp)], inc="P")
                    vmm2 = b.op("tensor", lambda e, k=k, scr=scr,
                                first=(k == 0), last=(k == NK - 1):
                                e.matmul(ps[bSX2][0:1, 0:TT], ones_bf[:, :], scr[:, :],
                                         start=first, stop=last), inc="P")
                    sqdep[k % 2] = vmm2
                    xmb_p[k % 2] = vmm1
                pst = b.cnt["P"]
                v1 = b.op("vector", lambda e: e.tensor_scalar(
                    out=stv(MU), in0=ps[bSX][0:1, 0:TT], scalar1=1.0 / D, scalar2=None,
                    op0=ALU.mult), deps=[("P", pst)], inc="V")
                v2 = b.op("vector", lambda e: e.tensor_scalar(
                    out=stv(M2), in0=ps[bSX2][0:1, 0:TT], scalar1=1.0 / D, scalar2=None,
                    op0=ALU.mult), deps=[("P", pst)], inc="V")
                bank_free[bSX] = ("V", v1)
                bank_free[bSX2] = ("V", v2)
                v3 = b.op("vector", lambda e: e.tensor_tensor(
                    out=stv(MSQ), in0=stv(MU), in1=stv(MU), op=ALU.mult),
                    deps=[("V", v2)], inc="V")
                v4 = b.op("vector", lambda e: e.scalar_tensor_tensor(
                    out=stv(VAR), in0=stv(M2), scalar=LN_EPS, in1=stv(MSQ),
                    op0=ALU.add, op1=ALU.subtract), deps=[("V", v3)], inc="V")
                v5 = b.op("scalar", lambda e: e.activation(
                    out=stv(SD), in_=stv(VAR), func=ACT.Sqrt),
                    deps=[("V", v4)], inc="SC")
                v6 = b.op("vector", lambda e: e.reciprocal(out=stv(RINV), in_=stv(SD)),
                          deps=[("SC", v5)], inc="V")
                vbm = b.op("tensor", lambda e: e.matmul(
                    ps[bBM][:, 0:TT], onesr[:, :], stv(MU), start=True, stop=True),
                    deps=[("V", v1), bank_free[bBM]], inc="P")
                vbr = b.op("tensor", lambda e: e.matmul(
                    ps[bBR][:, 0:TT], onesr[:, :], stv(RINV), start=True, stop=True),
                    deps=[("V", v6), bank_free[bBR]], inc="P")
                bank_ready[bBM], bank_ready[bBR] = vbm, vbr
                lastap = 0
                for k in range(NK):
                    lastap = ln_apply(
                        xmid[:, k * TO + t2 * TT:k * TO + t2 * TT + TT],
                        h2[:, k * TO + t2 * TT:k * TO + t2 * TT + TT],
                        bBM, bBR, [("P", max(vbm, vbr))],
                        cb1[:, k:k + 1], modpp[:, 3 * NK + k:3 * NK + k + 1],
                        f_scr[k % 2])
                bank_free[bBM] = bank_free[bBR] = ("V", lastap)

            for t2 in range(NTO):
                emit_ln3_tile(t2, (0, 1, 2, 3) if t2 % 2 == 0 else (4, 5, 6, 7))
            if c.debug:
                vv = b.cnt["V"]
                b.op("sync", lambda e: e.dma_start(out=dbg["dbg_h2"][:, :],
                     in_=h2[:, :]), deps=[("V", vv)], inc="HO", inc_by=16)
                b.need("sync", [("HO", b.cnt["HO"])])
            b.run_phase("p4_ln3")

        with contextlib.ExitStack() as ph:
            gub = [sb(ph, f"gub{i}", [128, 2 * NK * 128], BF16) for i in range(2)]
            dwb = sb(ph, "dwb", [128, c.IG * D], BF16)
            hh = sb(ph, "hh", [128, c.IG * TO], BF16)
            sil = [sb(ph, f"sil{i}", [128, TT], BF16) for i in range(2)]
            gu_done = {0: 0, 1: 0}
            dw_done = 0
            hh_free_P = 0
            sil_free = {0: 0, 1: 0}
            for g in range(c.NG):
                dwdeps = [("P", dw_done)]
                for il in range(c.IG):
                    i_abs = g * c.IG + il
                    b.op("gpsimd", lambda e, il=il, i_abs=i_abs: e.dma_start(
                        out=dwb[:, il * D:(il + 1) * D],
                        in_=dwT[i_abs * 128:(i_abs + 1) * 128, :]),
                        deps=dwdeps, inc="GD", inc_by=16)
                    dwdeps = []
                v_dw = b.cnt["GD"]
                for il in range(c.IG):
                    i_abs = g * c.IG + il
                    buf = gub[i_abs % 2]
                    gsem = "GA" if i_abs % 2 == 0 else "GB"
                    vg = b.op("gpsimd", lambda e, i_abs=i_abs, buf=buf: e.dma_start(
                        out=buf[:, 0:NK * 128].rearrange("p (k x) -> p k x", x=128),
                        in_=gwT[:, i_abs * 128:(i_abs + 1) * 128]
                        .rearrange("(k p) x -> p k x", p=128)),
                        deps=[("P", gu_done[i_abs % 2])], inc=gsem, inc_by=16)
                    vu = b.op("gpsimd", lambda e, i_abs=i_abs, buf=buf: e.dma_start(
                        out=buf[:, NK * 128:2 * NK * 128].rearrange("p (k x) -> p k x", x=128),
                        in_=upT[:, i_abs * 128:(i_abs + 1) * 128]
                        .rearrange("(k p) x -> p k x", p=128)),
                        inc=gsem, inc_by=16)
                    for t2 in range(NTO):
                        bg, bu = t2, 2 + t2
                        mmdeps = [(gsem, vu), bank_free[bg]]
                        vmg = None
                        for k in range(NK):
                            vmg = b.op("tensor", lambda e, k=k, buf=buf, bg=bg, t2=t2,
                                       first=(k == 0), last=(k == NK - 1):
                                       e.matmul(ps[bg][:, 0:TT],
                                                buf[:, k * 128:(k + 1) * 128],
                                                h2[:, k * TO + t2 * TT:k * TO + t2 * TT + TT],
                                                start=first, stop=last),
                                       deps=mmdeps, inc="P")
                            mmdeps = []
                        mmdeps = [bank_free[bu]]
                        vmu = None
                        for k in range(NK):
                            vmu = b.op("tensor", lambda e, k=k, buf=buf, bu=bu, t2=t2,
                                       first=(k == 0), last=(k == NK - 1):
                                       e.matmul(ps[bu][:, 0:TT],
                                                buf[:, NK * 128 + k * 128:NK * 128 + (k + 1) * 128],
                                                h2[:, k * TO + t2 * TT:k * TO + t2 * TT + TT],
                                                start=first, stop=last),
                                       deps=mmdeps, inc="P")
                            mmdeps = []
                        bank_ready[bg], bank_ready[bu] = vmg, vmu
                        gu_done[i_abs % 2] = vmu
                        scr = sil[t2 % 2]
                        vsl = b.op("scalar", lambda e, bg=bg, scr=scr: e.activation(
                            out=scr[:, :], in_=ps[bg][:, 0:TT], func=ACT.Silu),
                            deps=[("P", vmg), ("V", sil_free[t2 % 2])], inc="SC")
                        bank_free[bg] = ("SC", vsl)
                        vhh = b.op("vector", lambda e, il=il, t2=t2, bu=bu, scr=scr:
                                   e.tensor_tensor(
                                       out=hh[:, il * TO + t2 * TT:il * TO + t2 * TT + TT],
                                       in0=scr[:, :], in1=ps[bu][:, 0:TT], op=ALU.mult),
                                   deps=[("P", vmu), ("SC", vsl),
                                         ("P", hh_free_P)], inc="V")
                        bank_free[bu] = ("V", vhh)
                        sil_free[t2 % 2] = vhh
                v_hh = b.cnt["V"]
                last_dmm = 0
                for k in range(NK):
                    for t2 in range(NTO):
                        bank = 4 + (k * NTO + t2) % 4
                        mmdeps = [("V", v_hh), ("GD", v_dw), bank_free[bank]]
                        vmm = None
                        for il in range(c.IG):
                            vmm = b.op("tensor", lambda e, il=il, k=k, t2=t2, bank=bank,
                                       first=(il == 0), last=(il == c.IG - 1):
                                       e.matmul(ps[bank][:, 0:TT],
                                                dwb[:, il * D + k * 128:il * D + (k + 1) * 128],
                                                hh[:, il * TO + t2 * TT:il * TO + t2 * TT + TT],
                                                start=first, stop=last),
                                       deps=mmdeps, inc="P")
                            mmdeps = []
                        bank_ready[bank] = vmm
                        last_dmm = vmm
                        vev = b.op("vector", lambda e, k=k, t2=t2, bank=bank:
                                   e.scalar_tensor_tensor(
                                       out=xmid[:, k * TO + t2 * TT:k * TO + t2 * TT + TT],
                                       in0=ps[bank][:, 0:TT], scalar=gb[:, k:k + 1],
                                       in1=xmid[:, k * TO + t2 * TT:k * TO + t2 * TT + TT],
                                       op0=ALU.mult, op1=ALU.add),
                                   deps=[("P", vmm)], inc="V")
                        bank_free[bank] = ("V", vev)
                dw_done = last_dmm
                hh_free_P = last_dmm
            v_fin = b.cnt["V"]
            for k in range(NK):
                b.op("sync", lambda e, k=k: e.dma_start(
                    out=outT_h[k * 128:(k + 1) * 128, :],
                    in_=xmid[:, k * TO:(k + 1) * TO]),
                    deps=[("V", v_fin)], inc="HO", inc_by=16)
            b.need("sync", [("HO", b.cnt["HO"])])
            b.run_phase("p5_ffn")
        st_xm.close()
    return nc



def stage_inputs(c: Cfg, inputs):
    x = np.asarray(inputs["L0_attn_x"])
    D, T, S = c.D, c.T, c.S
    NK = c.NK

    def pp(v):
        return np.ascontiguousarray(np.asarray(v).reshape(NK, 128).T)

    shared = {
        "wshiftT": np.ascontiguousarray(np.asarray(inputs["L0_attn_W_shift"]).T),
        "wkeyT": np.ascontiguousarray(np.asarray(inputs["L0_attn_W_key"]).T),
        "wvalT": np.ascontiguousarray(np.asarray(inputs["L0_attn_W_value"]).T),
        "woutT": np.ascontiguousarray(np.asarray(inputs["L0_attn_W_output"]).T),
        "gwT": np.ascontiguousarray(np.asarray(inputs["L0_ffn_gate_w"]).T),
        "upT": np.ascontiguousarray(np.asarray(inputs["L0_ffn_up_w"]).T),
        "dwT": np.ascontiguousarray(np.asarray(inputs["L0_ffn_down_w"]).T),
        "wmod": np.ascontiguousarray(np.asarray(inputs["L0_wmod"])),
        "gatepp": pp(inputs["L0_attn_gate"]),
        "lngpp": pp(inputs["L0_attn_ln_gamma"]),
        "lnbpp": pp(inputs["L0_attn_ln_beta"]),
        "td": np.ascontiguousarray(np.asarray(inputs["L0_attn_time_decay"]).reshape(S, 1)),
        "tf": np.ascontiguousarray(np.asarray(inputs["L0_attn_time_first"]).reshape(S, 1)),
    }
    cmat = np.asarray(inputs["L0_c"])
    in_maps = []
    for core in range(8):
        bi, h = core // 2, core % 2
        xb = np.roll(x[bi], -c.TO * h, axis=0)
        m = dict(shared)
        m["xT"] = np.ascontiguousarray(xb.T)
        m["cpp"] = pp(cmat[bi])
        m["cmask"] = np.full((S, 1), float(h), np.float32)
        in_maps.append(m)
    return in_maps


_PROG_CACHE = {}


def kernel(**inputs):
    c = Cfg()
    if "prog" not in _PROG_CACHE:
        _PROG_CACHE["prog"] = build_program(c)
    nc = _PROG_CACHE["prog"]
    in_maps = stage_inputs(c, inputs)
    res = run_bass_kernel_spmd(nc, in_maps, core_ids=list(range(8)))
    out = np.empty((4, c.T, c.D), np.float32)
    for core in range(8):
        bi, h = core // 2, core % 2
        out[bi, h * c.TO:(h + 1) * c.TO, :] = res.results[core]["outT"].T
    return out
